# version 05 (final)
# baseline (speedup 1.0000x reference)
import contextlib
import math
import numpy as np
import ml_dtypes
import concourse.bass as bass
import concourse.mybir as mybir
from concourse.bass_utils import run_bass_kernel_spmd

F32 = mybir.dt.float32
BF16 = mybir.dt.bfloat16
AF = mybir.ActivationFunctionType
ALU = mybir.AluOpType
AX = mybir.AxisListType


class Cfg:
    def __init__(self, d=2048, inner=8192, tc=1024, nkv=4, gq=4,
                 ncores=8, theta=10000.0, softcap=50.0):
        self.d = d; self.inner = inner; self.tc = tc
        self.text = 2 * tc; self.win = tc
        self.nkv = nkv; self.gq = gq; self.nh = nkv * gq; self.hd = 128
        self.ncores = ncores; self.theta = theta; self.softcap = softcap
        assert d % 128 == 0 and inner % 128 == 0 and tc % 256 == 0
        self.nfc = d // 128
        self.noc = inner // 128
        self.ntq = tc // 128
        self.nte = self.text // 128
        self.gw = min(512, tc // 2)
        assert (tc // 2) % self.gw == 0
        self.ngrp_ext = self.text // self.gw
        self.nsig = self.win // 128 + 1
        self.dq = self.nh * self.hd
        self.dkv = self.nkv * self.hd
        self.q1 = 128 // self.nfc
        self.q2 = 128 // self.noc
        assert 128 % self.nfc == 0 and 128 % self.noc == 0
        self.hw = tc // 2


def hadamard(n):
    h = np.array([[1.0]])
    while h.shape[0] < n:
        h = np.block([[h, h], [h, -h]])
    return h.astype(np.float32)


def host_prep(cfg, core, x, wq, wk, wv, wo, wg, wu, wd):
    c = cfg
    t0 = core * c.tc
    inm = {}
    xe = np.zeros((c.text, c.d), dtype=np.float32)
    lo = max(0, t0 - c.tc)
    xe[c.tc - (t0 - lo):] = x[lo:t0 + c.tc]
    inm["x_ext"] = np.ascontiguousarray(
        xe.T.reshape(c.nfc, 128, c.ngrp_ext, c.gw).transpose(2, 1, 0, 3))

    def strips(w):
        fi, fo = w.shape
        return np.ascontiguousarray(
            w.reshape(fi // 128, 128, fo // 128, 128).transpose(2, 1, 0, 3)
            .reshape(fo // 128, 128, fi))
    inm["wq"] = strips(wq)
    inm["wk"] = strips(wk)
    inm["wo"] = strips(wo)
    inm["wv"] = np.ascontiguousarray(
        wv.reshape(c.nfc, 128, c.dkv).transpose(1, 0, 2).reshape(128, c.nfc * c.dkv))
    mu = np.arange(c.d)
    j1, rem = mu // 128, mu % 128
    e, cp = rem // c.nfc, rem % c.nfc
    fprime = cp * 128 + j1 * c.q1 + e
    inm["wg"] = strips(np.ascontiguousarray(wg.T)[fprime])
    inm["wu"] = strips(np.ascontiguousarray(wu.T)[fprime])
    mu2 = np.arange(c.inner)
    j2, rem2 = mu2 // 128, mu2 % 128
    b, cp2 = rem2 // c.noc, rem2 % c.noc
    hprime = cp2 * 128 + j2 * c.q2 + b
    inm["wdn"] = strips(np.ascontiguousarray(wd.T)[hprime])
    pos = np.arange(t0 - c.tc, t0 + c.tc, dtype=np.float64)
    frac = np.arange(0, c.hd, 2, dtype=np.float64) / c.hd
    invt = 1.0 / (c.theta ** frac)
    ang = pos[None, :] * np.concatenate([invt, invt])[:, None]
    sin = np.sin(ang); cos = np.cos(ang)
    inm["cos_k"] = cos.astype(np.float32)
    inm["sin_k"] = np.concatenate(
        [-sin[: c.hd // 2], sin[c.hd // 2:]], axis=0).astype(np.float32)
    ii = np.arange(128)[:, None]; jj = np.arange(128)[None, :]
    inm["tri_c"] = (ii <= jj).astype(np.float32)
    inm["tri_w"] = (ii >= jj).astype(np.float32)
    rb = np.zeros((128, c.ntq * c.nsig), dtype=np.float32)
    for tau in range(c.ntq):
        for sig in range(c.nsig):
            s0 = t0 + (tau - c.ntq + sig) * 128
            rb[:, tau * c.nsig + sig] = np.where(
                np.arange(s0, s0 + 128) < 0, -30.0, 0.0)
    inm["rbias"] = rb
    inm["ident"] = np.eye(128, dtype=np.float32)
    inm["h128"] = hadamard(128)
    inm["hb1"] = np.kron(np.eye(c.q1), hadamard(c.nfc)).astype(np.float32)
    inm["hb2"] = np.kron(np.eye(c.q2), hadamard(c.noc)).astype(np.float32)
    return inm


class Prog:
    ENG = ("sync", "gpsimd", "tensor", "vector", "scalar")

    def __init__(self, nc):
        self.nc = nc
        self.cmds = {k: [] for k in self.ENG}

    def emit(self, eng, fn):
        self.cmds[eng].append(fn)

    def flush(self):
        if not any(self.cmds.values()):
            return
        with self.nc.Block() as blk:
            for name in self.ENG:
                fns = self.cmds[name]
                if fns:
                    getattr(blk, name)(lambda e, fns=fns: [f(e) for f in fns])
        self.cmds = {k: [] for k in self.ENG}


class Sem:
    def __init__(self, es, nc, name):
        self.h = es.enter_context(nc.semaphore(name))
        self.n = 0

    def bump(self, k=1):
        self.n += k
        return self.n


def _th(t):
    return t.tensor if isinstance(t, bass.AP) else t


def ap2(t, off, d0, d1):
    return bass.AP(_th(t), off, [list(d0), list(d1)])


def ap3(t, off, d0, d1, d2):
    return bass.AP(_th(t), off, [list(d0), list(d1), list(d2)])


def ap4(t, off, d0, d1, d2, d3):
    return bass.AP(_th(t), off, [list(d0), list(d1), list(d2), list(d3)])


class Builder:

    def __init__(self, cfg, dbg=()):
        self.c = cfg
        self.dbg = set(dbg)
        self.nc = bass.Bass("TRN2", target_bir_lowering=False, debug=False,
                            detect_race_conditions=False)
        self.es = contextlib.ExitStack()
        self.P = Prog(self.nc)
        self.sS = Sem(self.es, self.nc, "sS")
        self.sG = Sem(self.es, self.nc, "sG")
        self.sT = Sem(self.es, self.nc, "sT")
        self.sV = Sem(self.es, self.nc, "sV")
        self.sA = Sem(self.es, self.nc, "sA")
        self.dram = {}
        self.sb = {}
        self.ps = {}
        self.ps_free = {}
        self.buf_free = {}
        self._plan = []

    def din(self, name, shape):
        self.dram[name] = self.nc.dram_tensor(name, list(shape), F32,
                                              kind="ExternalInput").ap()
        return self.dram[name]

    def dout(self, name, shape, dt=F32):
        self.dram[name] = self.nc.dram_tensor(name, list(shape), dt,
                                              kind="ExternalOutput").ap()
        return self.dram[name]

    def dtmp(self, name, shape, dt=F32):
        self.dram[name] = self.nc.dram_tensor(name, list(shape), dt,
                                              kind="Internal").ap()
        return self.dram[name]

    def plan(self, name, shape, dt=BF16, live=(0, 3)):
        self._plan.append((name, list(shape), dt, live[0], live[1]))

    def allocate(self):
        BASE, TOP = 16512, 229344
        items = []
        for name, shape, dt, p0, p1 in self._plan:
            nb = 1
            for s in shape[1:]:
                nb *= s
            nb *= mybir.dt.size(dt)
            nb = (nb + 31) // 32 * 32
            items.append((nb, name, shape, dt, p0, p1))
        items.sort(key=lambda it: -it[0])
        placed = []
        for nb, name, shape, dt, p0, p1 in items:
            off = BASE
            moved = True
            while moved:
                moved = False
                for (o2, e2, q0, q1x, nm2) in placed:
                    if not (q1x < p0 or q0 > p1):
                        if not (off + nb <= o2 or off >= e2):
                            off = e2
                            moved = True
            assert off + nb <= TOP, (
                f"SBUF overflow placing {name} ({nb}B at {off})")
            placed.append((off, off + nb, p0, p1, name))
            self.sb[name] = self.nc.alloc_sbuf_tensor_at(
                name, shape, dt, offset=off)
        self.sbuf_peak = max(e for (_, e, _, _, _) in placed)

    def psum(self, name, shape, dt=F32):
        h = self.es.enter_context(self.nc.psum_tensor(name, list(shape), dt))
        self.ps[name] = h
        return h

    def w(self, eng, sem, v):
        if v > 0:
            self.P.emit(eng, lambda e, s=sem.h, v=v: e.wait_ge(s, v))

    def dma(self, q, out, in_, wait=(), **kw):
        sem = self.sS if q == "sync" else self.sG
        for (ws, wv) in wait:
            self.w(q, ws, wv)
        v = sem.bump(16)
        self.P.emit(q, lambda e, o=out, i=in_, s=sem.h, kw=kw:
                    e.dma_start(out=o, in_=i, **kw).then_inc(s, 16))
        return v

    def op(self, eng, build, wait=()):
        sem = {"tensor": self.sT, "vector": self.sV, "scalar": self.sA}[eng]
        for (ws, wv) in wait:
            self.w(eng, ws, wv)
        v = sem.bump(1)
        self.P.emit(eng, lambda e, b=build, s=sem.h: b(e).then_inc(s, 1))
        return v

    def opn(self, eng, build, wait=()):
        for (ws, wv) in wait:
            self.w(eng, ws, wv)
        self.P.emit(eng, lambda e, b=build: b(e))

    def mm(self, psname, out_ap, lhsT, rhs, start, stop, wait=()):
        wl = list(wait)
        if start and psname in self.ps_free:
            wl.append(self.ps_free[psname])
        return self.op("tensor", lambda e, o=out_ap, l=lhsT, r=rhs, s=start,
                       p=stop: e.matmul(o, l, r, start=s, stop=p), wait=tuple(wl))

    def ps_release(self, psname, sem, cnt):
        self.ps_free[psname] = (sem, cnt)

    def mm_raw(self, psname, build, wait=()):
        wl = list(wait)
        if psname in self.ps_free:
            wl.append(self.ps_free[psname])
        return self.op("tensor", build, wait=tuple(wl))

    def dump(self, name, src_ap, shape):
        if name not in self.dbg:
            return
        h = self.dout("dbg_" + name, shape)
        v = self.dma("gpsimd", h[:], src_ap,
                     wait=((self.sV, self.sV.n), (self.sA, self.sA.n),
                           (self.sT, self.sT.n), (self.sS, self.sS.n)))
        for eng in ("tensor", "vector", "scalar", "sync", "gpsimd"):
            self.w(eng, self.sG, v)

    def barrier(self):
        vs = {}
        for eng, sem in (("tensor", self.sT), ("vector", self.sV),
                         ("scalar", self.sA)):
            v = sem.bump(1)
            self.P.emit(eng, lambda e, s=sem.h: e.nop().then_inc(s, 1))
            vs[eng] = (sem, v)
        for eng in Prog.ENG:
            for oeng, (sem, v) in vs.items():
                if oeng != eng:
                    self.w(eng, sem, v)
            self.w(eng, self.sS, self.sS.n)
            self.w(eng, self.sG, self.sG.n)


class KBuilder(Builder):
    def declare(self):
        c = self.c
        self.din("x_ext", (c.ngrp_ext, 128, c.nfc, c.gw))
        self.din("wq", (c.dq // 128, 128, c.d))
        self.din("wk", (c.dkv // 128, 128, c.d))
        self.din("wo", (c.d // 128, 128, c.dq))
        self.din("wv", (128, c.nfc * c.dkv))
        self.din("wg", (c.noc, 128, c.d))
        self.din("wu", (c.noc, 128, c.d))
        self.din("wdn", (c.nfc, 128, c.inner))
        self.din("cos_k", (128, c.text)); self.din("sin_k", (128, c.text))
        self.din("tri_c", (128, 128)); self.din("tri_w", (128, 128))
        self.din("rbias", (128, c.ntq * c.nsig))
        self.din("ident", (128, 128))
        self.din("h128", (128, 128)); self.din("hb1", (128, 128))
        self.din("hb2", (128, 128))
        self.dout("out", (c.d, c.tc))
        self.dtmp("x2park", (128, c.nfc * c.tc), BF16)
        self.dtmp("zApark", (128, c.noc * c.tc), BF16)
        self.dtmp("zBpark", (128, c.nfc * c.tc), BF16)
        self.dtmp("colpark", (128 * max(c.ntq, 8),), F32)

    def plan_all(self):
        c = self.c
        P = self.plan
        NW = c.gq * 128
        for nm in ("tri_c", "tri_w", "ident", "h128", "hb1", "hb2"):
            P("c_" + nm, [128, 128], BF16, (0, 3))
        P("c_rbias", [128, c.ntq * c.nsig], F32, (0, 3))
        P("ones_col", [128, 1], BF16, (0, 3))
        P("ones_col_f32", [128, 1], F32, (0, 3))
        P("ones_row", [1, 128], BF16, (0, 3))
        P("ones_row_f32", [1, 128], F32, (0, 3))
        P("row_t0", [1, 512], F32, (0, 3))
        P("row_t1", [1, 512], F32, (0, 3))
        P("arow_m", [1, 512], F32, (0, 3))
        P("arow_r", [1, 512], F32, (0, 3))
        P("sq0", [128, 512], BF16, (0, 3))
        P("sq1", [128, 512], BF16, (0, 3))
        P("t1a", [128, 512], F32, (0, 3))
        P("t1b", [128, 512], F32, (0, 3))
        P("t1c", [128, 512], BF16, (0, 3))
        P("mbc", [128, 512], BF16, (0, 3))
        P("rbc", [128, 512], BF16, (0, 3))
        P("fs0", [128, 512], F32, (0, 3))
        P("fs1", [128, 512], F32, (0, 3))
        P("wsc", [1, 12], F32, (0, 3))
        P("thr", [128, 8], F32, (0, 3))
        P("col_t", [128, 2], F32, (0, 3))
        P("cstb", [128, 2], F32, (0, 3))
        for wname, ns, sw in (("wg", c.noc, c.d), ("wu", c.noc, c.d),
                              ("wdn", c.nfc, c.inner)):
            P("am_" + wname, [128, (ns * sw) // 512], F32, (0, 3))
        P("xg0", [128, c.nfc * c.gw], BF16, (0, 0))
        P("xg1", [128, c.nfc * c.gw], BF16, (0, 0))
        P("xn0", [128, c.nfc * c.gw], BF16, (0, 0))
        P("xn1", [128, c.nfc * c.gw], BF16, (0, 0))
        P("wvs", [128, c.nfc * c.dkv], BF16, (0, 0))
        P("k_fm", [128, (c.dkv // 128) * c.text], BF16, (0, 0))
        P("v_tm", [128, c.nte * c.dkv], BF16, (0, 0))
        P("q_h", [128, c.nh * c.hw], BF16, (0, 0))
        P("enc_h", [128, c.nh * c.hw], BF16, (0, 0))
        P("rcos_k", [128, c.text], BF16, (0, 0))
        P("rsin_k", [128, c.text], BF16, (0, 0))
        P("ws0", [128, max(c.d, c.dq)], BF16, (0, 0))
        P("ws1", [128, max(c.d, c.dq)], BF16, (0, 0))
        P("mrow", [1, c.text], BF16, (0, 0))
        P("rrow", [1, c.text], BF16, (0, 0))
        P("pp0", [128, NW], BF16, (0, 0))
        P("pp1", [128, NW], BF16, (0, 0))
        P("tanh0", [128, NW], F32, (0, 0))
        P("rbcast", [128, NW], BF16, (0, 0))
        P("drow", [1, NW], F32, (0, 0))
        P("fx2", [128, c.nfc * c.tc], BF16, (1, 1))
        P("fxn2", [128, c.nfc * c.tc], BF16, (1, 1))
        P("fv2", [128, c.nfc * c.tc], BF16, (1, 1))
        P("fxq1", [128, c.nfc * c.tc], BF16, (1, 2))
        P("bcg", [128, c.tc], BF16, (1, 2))
        P("bcu", [128, c.tc], BF16, (1, 2))
        P("fa1", [128, c.d], F32, (1, 1))
        P("fi1", [128, c.d], mybir.dt.int32, (1, 1))
        P("q1mx", [128, c.ntq * c.nfc], F32, (1, 1))
        P("q1c", [128, c.ntq], F32, (1, 1))
        P("q1sc", [128, c.ntq], F32, (1, 1))
        P("q1dq", [128, c.ntq], F32, (1, 1))
        P("frow_m", [1, c.tc], F32, (1, 1))
        P("frow_r", [1, c.tc], F32, (1, 1))
        P("r1row", [1, c.tc], F32, (1, 1))
        P("r1g", [1, c.tc], F32, (1, 1))
        P("r1u", [1, c.tc], F32, (1, 1))
        P("gs0", [128, c.d], BF16, (2, 2)); P("gs1", [128, c.d], BF16, (2, 2))
        P("us0", [128, c.d], BF16, (2, 2)); P("us1", [128, c.d], BF16, (2, 2))
        P("tmpc", [128, c.d], BF16, (2, 2))
        P("fgl", [128, c.tc], F32, (2, 2))
        P("fgl2", [128, c.tc], F32, (2, 2))
        P("fsl", [128, c.tc], BF16, (2, 2))
        P("fu", [128, c.tc], BF16, (2, 2))
        P("zst", [128, 512], BF16, (2, 2))
        P("fv8", [128, c.noc * 128], BF16, (3, 3))
        P("fz8", [128, c.noc * 128], BF16, (3, 3))
        P("fa2", [128, 1024], F32, (3, 3))
        P("fi2", [128, 1024], mybir.dt.int32, (3, 3))
        P("fxq2", [128, c.noc * c.hw], BF16, (3, 3))
        P("q2mx", [128, 128 // c.q2], F32, (3, 3))
        P("q2c", [128, 8], F32, (3, 3))
        P("q2sc", [128, 8], F32, (3, 3))
        P("q2dq", [128, 8], F32, (3, 3))
        P("ds0", [128, c.inner // 2], BF16, (3, 3))
        P("ds1", [128, c.inner // 2], BF16, (3, 3))
        P("tmpd", [128, c.inner // 2], BF16, (3, 3))
        P("fy", [128, c.nfc * c.hw], BF16, (3, 3))
        P("fx2h", [128, c.nfc * c.hw], BF16, (3, 3))
        P("fout", [128, 2 * c.hw], F32, (3, 3))
        P("r2row", [1, c.hw], F32, (3, 3))
        P("bc2", [128, c.hw], BF16, (3, 3))

    def build(self, stage=99):
        c = self.c
        self.declare()
        self.plan_all()
        self.allocate()
        self.psum("pA", [128, 512]); self.psum("pB", [128, 512])
        self.psum("pC", [128, 512]); self.psum("pD", [128, 512])
        self.psum("pE", [128, 512]); self.psum("pF", [128, 512])
        self.psum("pR", [33, 512])
        self.nc.const_aps.register(F32, 128.5, self.sb["cstb"][:, 1:2])
        self.absmean_init()
        self.phaseA_consts()
        if stage >= 1:
            self.phaseA_lnkv()
        self.P.flush()
        if stage >= 2:
            for h in range(2):
                self.phaseA_half(h)
                self.P.flush()
        self.barrier()
        self.P.flush()
        if stage >= 3:
            self.phaseF(stage)
        self.barrier()
        self.P.flush()
        return self.nc

    def absmean_init(self):
        c = self.c
        self.CK = 512
        self.jobs = []
        self.fs_free = [(), ()]
        self.am_nck = {}
        for wname, ns, sw in (("wg", c.noc, c.d), ("wu", c.noc, c.d),
                              ("wdn", c.nfc, c.inner)):
            per_strip = sw // self.CK
            nck = ns * per_strip
            self.am_nck[wname] = nck
            for ck in range(nck):
                self.jobs.append((wname, ck // per_strip,
                                  (ck % per_strip) * self.CK, ck))
        self.job_i = 0

    def pump(self, k):
        sb, dr = self.sb, self.dram
        while k > 0 and self.job_i < len(self.jobs):
            wname, st, off, ck = self.jobs[self.job_i]
            b = self.job_i % 2
            fs = sb["fs0" if b == 0 else "fs1"]
            acc = sb["am_" + wname]
            ld = self.dma("sync", fs[:], dr[wname][st][:, off:off + self.CK],
                          wait=self.fs_free[b])
            va = self.op("scalar", lambda e, fs=fs, acc=acc, ck=ck: e.activation(
                out=fs[:], in_=fs[:], func=AF.Abs,
                accum_out=acc[:, ck:ck + 1]), wait=((self.sS, ld),))
            self.fs_free[b] = ((self.sA, va),)
            self.job_i += 1
            k -= 1

    def phaseA_consts(self):
        c, sb, dr = self.c, self.sb, self.dram
        for nm in ("tri_c", "tri_w", "ident", "h128", "hb1", "hb2"):
            self.dma("gpsimd", sb["c_" + nm][:], dr[nm][:])
        self.dma("sync", sb["c_rbias"][:], dr["rbias"][:])
        self.dma("gpsimd", sb["rcos_k"][:], dr["cos_k"][:])
        self.dma("gpsimd", sb["rsin_k"][:], dr["sin_k"][:])
        self.dma("gpsimd", sb["wvs"][:], dr["wv"][:])
        self.opn("vector", lambda e: e.memset(sb["cstb"][:, 0:1], -float(c.softcap)))
        self.opn("vector", lambda e: e.memset(sb["cstb"][:, 1:2], 128.5))
        self.opn("vector", lambda e: e.memset(sb["ones_col"][:], 1.0))
        self.opn("vector", lambda e: e.memset(sb["ones_col_f32"][:], 1.0))
        self.opn("vector", lambda e: e.memset(sb["ones_row"][:], 1.0))
        self.v_consts = self.op("vector", lambda e: e.memset(
            sb["ones_row_f32"][:], 1.0))
        self.g_consts = self.sG.n

    def ln_stats(self, src, width, nch, cstride, soff, mdst, rdst, wait=()):
        sb = self.sb
        d = 128 * nch
        vm = None
        for fc in range(nch):
            s = sb["sq0" if fc % 2 == 0 else "sq1"]
            st = (fc == 0); sp = (fc == nch - 1)
            wv = (tuple(wait) if fc == 0 else ()) + (
                (self.sT, self.buf_free.get("sq" + str(fc % 2), 0)),)
            vsq = self.op("scalar", lambda e, fc=fc, s=s: e.activation(
                out=s[:, 0:width],
                in_=src[:, fc * cstride + soff:fc * cstride + soff + width],
                func=AF.Square), wait=wv)
            self.mm("pR", self.ps["pR"][0:1, 0:width], sb["ones_col"][:],
                    src[:, fc * cstride + soff:fc * cstride + soff + width],
                    st, sp, wait=tuple(wait) if fc == 0 else ())
            vm = self.mm("pR", self.ps["pR"][32:33, 0:width], sb["ones_col"][:],
                         s[:, 0:width], st, sp, wait=((self.sA, vsq),))
            self.buf_free["sq" + str(fc % 2)] = vm
        t0, t1 = sb["row_t0"], sb["row_t1"]
        vms = self.op("vector", lambda e: e.tensor_scalar(
            out=mdst, in0=self.ps["pR"][0:1, 0:width], scalar1=1.0 / d,
            scalar2=None, op0=ALU.mult), wait=((self.sT, vm),))
        vxx = self.op("vector", lambda e: e.tensor_scalar(
            out=t0[:, 0:width], in0=self.ps["pR"][32:33, 0:width], scalar1=1.0 / d,
            scalar2=None, op0=ALU.mult))
        self.ps_release("pR", self.sV, vxx)
        self.opn("vector", lambda e: e.tensor_tensor(
            out=t1[:, 0:width], in0=mdst, in1=mdst, op=ALU.mult))
        self.opn("vector", lambda e: e.scalar_tensor_tensor(
            out=t0[:, 0:width], in0=t0[:, 0:width], scalar=1e-5,
            in1=t1[:, 0:width], op0=ALU.add, op1=ALU.subtract))
        vv = self.op("vector", lambda e: e.reciprocal(
            out=t1[:, 0:width], in_=t0[:, 0:width]))
        return self.op("scalar", lambda e: e.activation(
            out=rdst, in_=t1[:, 0:width], func=AF.Sqrt), wait=((self.sV, vv),))

    def bcast_pair(self, mrow_ap, rrow_ap, width, f32, wait=()):
        sb = self.sb
        ones = sb["ones_row_f32"] if f32 else sb["ones_row"]
        v0 = self.mm("pE", self.ps["pE"][:, 0:width], ones[:], mrow_ap,
                     True, True, wait=wait)
        v1 = self.mm("pF", self.ps["pF"][:, 0:width], ones[:], rrow_ap,
                     True, True)
        self.op("vector", lambda e: e.tensor_copy(
            out=sb["mbc"][:, 0:width], in_=self.ps["pE"][:, 0:width]),
            wait=((self.sT, v0),))
        vb = self.op("vector", lambda e: e.tensor_copy(
            out=sb["rbc"][:, 0:width], in_=self.ps["pF"][:, 0:width]),
            wait=((self.sT, v1),))
        self.ps_release("pE", self.sV, vb)
        self.ps_release("pF", self.sV, vb)
        return vb

    def ln_apply(self, src, dst, width, nch, cstride, soff, wait=()):
        sb = self.sb
        v = None
        for fc in range(nch):
            self.opn("vector", lambda e, fc=fc: e.tensor_tensor(
                out=sb["t1c"][:, 0:width],
                in0=src[:, fc * cstride + soff:fc * cstride + soff + width],
                in1=sb["mbc"][:, 0:width], op=ALU.subtract),
                wait=tuple(wait) if fc == 0 else ())
            v = self.op("vector", lambda e, fc=fc: e.tensor_tensor(
                out=dst[:, fc * cstride + soff:fc * cstride + soff + width],
                in0=sb["t1c"][:, 0:width], in1=sb["rbc"][:, 0:width],
                op=ALU.mult))
        return v

    def rope_evac(self, psname, dst_t, dst_off, W, poff):
        sb = self.sb
        ps = self.ps[psname]
        t1, t2 = sb["t1a"], sb["t1b"]
        cosT, sinT = sb["rcos_k"], sb["rsin_k"]
        wtmm = ((self.sT, self.sT.n),)
        self.opn("vector", lambda e: e.tensor_tensor(
            out=t1[:, 0:W], in0=ps[:, 0:W], in1=cosT[:, poff:poff + W],
            op=ALU.mult), wait=wtmm)
        self.opn("vector", lambda e: e.tensor_tensor(
            out=t2[0:64, 0:W], in0=ps[64:128, 0:W], in1=sinT[0:64, poff:poff + W],
            op=ALU.mult))
        self.opn("vector", lambda e: e.tensor_tensor(
            out=t2[64:128, 0:W], in0=ps[0:64, 0:W], in1=sinT[64:128, poff:poff + W],
            op=ALU.mult))
        return self.op("vector", lambda e: e.tensor_tensor(
            out=ap2(dst_t, dst_off, (dst_t.shape[1], 128), (1, W)),
            in0=t1[:, 0:W], in1=t2[:, 0:W], op=ALU.add))

    def phaseA_lnkv(self):
        c, sb, dr = self.c, self.sb, self.dram
        xgb = ("xg0", "xg1"); xnb = ("xn0", "xn1")
        nkvc = c.dkv // 128
        wsb = ("ws0", "ws1")
        for g in range(c.ngrp_ext):
            self.pump(20)
            xg = sb[xgb[g % 2]]; xn = sb[xnb[g % 2]]
            ld = self.dma("gpsimd", xg[:], dr["x_ext"][g],
                          wait=self.buf_free.get(xgb[g % 2], ()))
            wld = ((self.sG, ld), (self.sV, self.v_consts))
            vr = self.ln_stats(xg, c.gw, c.nfc, c.gw, 0,
                               sb["mrow"][:, g * c.gw:(g + 1) * c.gw],
                               sb["rrow"][:, g * c.gw:(g + 1) * c.gw], wait=wld)
            self.bcast_pair(sb["mrow"][:, g * c.gw:(g + 1) * c.gw],
                            sb["rrow"][:, g * c.gw:(g + 1) * c.gw], c.gw,
                            f32=False, wait=((self.sA, vr),))
            xn_prev = self.buf_free.get(xnb[g % 2], 0)
            assert isinstance(xn_prev, int)
            xn_ok = self.ln_apply(xg, xn, c.gw, c.nfc, c.gw, 0,
                                  wait=((self.sT, xn_prev),))
            self.dump("rrow_g%d" % g, self.sb["rrow"][:, 0:(g + 1) * c.gw],
                      (1, (g + 1) * c.gw))
            self.buf_free[xgb[g % 2]] = (
                (self.sT, self.sT.n), (self.sA, self.sA.n), (self.sV, xn_ok))
            for oc in range(nkvc):
                b = oc % 2
                wl = self.dma("gpsimd", sb[wsb[b]][:, 0:c.d], dr["wk"][oc],
                              wait=self.buf_free.get(wsb[b], ()))
                pn = "pA" if oc % 2 == 0 else "pB"
                for fc in range(c.nfc):
                    w0 = ((self.sV, xn_ok), (self.sG, wl)) if fc == 0 else ()
                    self.mm(pn, self.ps[pn][:, 0:c.gw],
                            sb[wsb[b]][:, fc * 128:(fc + 1) * 128],
                            xn[:, fc * c.gw:(fc + 1) * c.gw],
                            fc == 0, fc == c.nfc - 1, wait=w0)
                vk = self.rope_evac(pn, sb["k_fm"], oc * c.text + g * c.gw,
                                    c.gw, g * c.gw)
                self.ps_release(pn, self.sV, vk)
                self.buf_free[wsb[b]] = ((self.sT, self.sT.n),)
            nst = c.gw // 128
            for stl in range(nst):
                s_tile = g * nst + stl
                pn = "pC" if stl % 2 == 0 else "pD"
                for fc in range(c.nfc):
                    self.mm(pn, self.ps[pn][:, 0:c.dkv],
                            xn[:, fc * c.gw + stl * 128:fc * c.gw + stl * 128 + 128],
                            sb["wvs"][:, fc * c.dkv:(fc + 1) * c.dkv],
                            fc == 0, fc == c.nfc - 1,
                            wait=((self.sG, self.g_consts),)
                            if (g == 0 and stl == 0 and fc == 0) else ())
                vv = self.op("vector", lambda e, s_tile=s_tile, pn=pn: e.tensor_copy(
                    out=sb["v_tm"][:, s_tile * c.dkv:(s_tile + 1) * c.dkv],
                    in_=self.ps[pn][:, 0:c.dkv]), wait=((self.sT, self.sT.n),))
                self.ps_release(pn, self.sV, vv)
            self.buf_free[xnb[g % 2]] = self.sT.n
            if g == c.ngrp_ext - 1:
                self.dump("xg_l", xg[:], (128, c.nfc * c.gw))
                self.dump("xn_l", xn[:], (128, c.nfc * c.gw))
        self.dump("mrow", self.sb["mrow"][:], (1, c.text))
        self.dump("rrow", self.sb["rrow"][:], (1, c.text))
        self.buf_free["xn0_rd"] = ((self.sT, self.sT.n),)
        self.buf_free["x2h_rd"] = ((self.sT, self.sT.n),)
        self.dump("k_fm", self.sb["k_fm"][:], (128, (self.c.dkv // 128) * self.c.text))
        self.dump("v_tm", self.sb["v_tm"][:], (128, self.c.nte * self.c.dkv))

    def phaseA_half(self, h):
        c, sb, dr = self.c, self.sb, self.dram
        hw = c.hw
        assert hw == c.gw
        ntq2 = c.ntq // 2
        NW = c.gq * 128
        xgb = ("xg0", "xg1"); xnb = ("xn0", "xn1")
        wsb = ("ws0", "ws1")
        g_ext = (c.tc + h * hw) // c.gw
        xg = sb[xgb[0]]; xn = sb[xnb[0]]
        ld = self.dma("gpsimd", xg[:], dr["x_ext"][g_ext],
                      wait=self.buf_free.get(xgb[0], ()))
        self.buf_free[xgb[0]] = ()
        poff = c.tc + h * hw
        self.bcast_pair(sb["mrow"][:, poff:poff + hw],
                        sb["rrow"][:, poff:poff + hw], hw, f32=False,
                        wait=((self.sG, ld),))
        xn_ok = self.ln_apply(xg, xn, hw, c.nfc, c.gw, 0,
                              wait=self.buf_free.get("xn0_rd", ()))
        for oc in range(c.nh):
            self.pump(6)
            b = oc % 2
            wl = self.dma("gpsimd", sb[wsb[b]][:, 0:c.d], dr["wq"][oc],
                          wait=self.buf_free.get(wsb[b], ()))
            pn = "pA" if oc % 2 == 0 else "pB"
            for fc in range(c.nfc):
                w0 = ((self.sG, wl), (self.sV, xn_ok)) if fc == 0 else ()
                self.mm(pn, self.ps[pn][:, 0:hw],
                        sb[wsb[b]][:, fc * 128:(fc + 1) * 128],
                        xn[:, fc * c.gw:fc * c.gw + hw],
                        fc == 0, fc == c.nfc - 1, wait=w0)
            vq = self.rope_evac(pn, sb["q_h"], oc * hw, hw, poff)
            self.ps_release(pn, self.sV, vq)
            self.buf_free[wsb[b]] = ((self.sT, self.sT.n),)
        q_done = self.sV.n
        self.dump("q%d" % h, sb["q_h"][:], (128, c.nh * hw))
        ppb = ("pp0", "pp1")
        tansc = (c.hd ** -0.5) / c.softcap
        for tl in range(ntq2):
            tau = h * ntq2 + tl
            for kv in range(c.nkv):
                self.pump(6)
                exp_cnt = {}
                for sig in range(c.nsig + 1):
                    if sig < c.nsig:
                        e_tile = tau + sig
                        pn = "pA" if sig % 2 == 0 else "pB"
                        vsc = self.mm(
                            pn, self.ps[pn][:, 0:NW],
                            sb["k_fm"][:, kv * c.text + e_tile * 128:
                                       kv * c.text + e_tile * 128 + 128],
                            ap3(sb["q_h"], (kv * c.gq) * hw + tl * 128,
                                (hw * c.nh, 128), (hw, c.gq), (1, 128)),
                            True, True,
                            wait=((self.sV, q_done),)
                            if (sig == 0 and kv == 0 and tl == 0) else ())
                        rcol = sb["c_rbias"][:, tau * c.nsig + sig:
                                             tau * c.nsig + sig + 1]
                        vth = self.op("scalar", lambda e, pn=pn, rcol=rcol: e.activation(
                            out=sb["tanh0"][:, 0:NW], in_=self.ps[pn][:, 0:NW],
                            func=AF.Tanh, scale=tansc, bias=rcol),
                            wait=((self.sT, vsc),))
                        self.ps_release(pn, self.sA, vth)
                        vex = self.op("scalar", lambda e, sig=sig: e.activation(
                            out=sb[ppb[sig % 2]][:, 0:NW],
                            in_=sb["tanh0"][:, 0:NW], func=AF.Exp,
                            scale=float(c.softcap), bias=sb["cstb"][:, 0:1]),
                            wait=self.buf_free.get(ppb[sig % 2], ()))
                        if sig == 0 or sig == c.nsig - 1:
                            tri = sb["c_tri_w"] if sig == 0 else sb["c_tri_c"]
                            vex = self.op("vector", lambda e, sig=sig, tri=tri:
                                          e.tensor_tensor(
                                out=ap3(sb[ppb[sig % 2]], 0, (NW, 128),
                                        (128, c.gq), (1, 128)),
                                in0=ap3(sb[ppb[sig % 2]], 0, (NW, 128),
                                        (128, c.gq), (1, 128)),
                                in1=ap3(tri, 0, (128, 128), (0, c.gq), (1, 128)),
                                op=ALU.mult), wait=((self.sA, vex),))
                            exp_cnt[sig] = (self.sV, vex)
                        else:
                            exp_cnt[sig] = (self.sA, vex)
                    if sig >= 1:
                        s2 = sig - 1
                        e2 = tau + s2
                        self.mm("pC", self.ps["pC"][:, 0:NW],
                                sb["v_tm"][:, e2 * c.dkv + kv * 128:
                                           e2 * c.dkv + kv * 128 + 128],
                                sb[ppb[s2 % 2]][:, 0:NW],
                                s2 == 0, s2 == c.nsig - 1,
                                wait=(exp_cnt[s2],))
                        vrw = self.mm("pR", self.ps["pR"][0:1, 0:NW],
                                      sb["ones_col"][:], sb[ppb[s2 % 2]][:, 0:NW],
                                      s2 == 0, s2 == c.nsig - 1)
                        self.buf_free[ppb[s2 % 2]] = ((self.sT, vrw),)
                vdr = self.op("vector", lambda e: e.tensor_copy(
                    out=sb["drow"][:, 0:NW], in_=self.ps["pR"][0:1, 0:NW]),
                    wait=((self.sT, self.sT.n),))
                self.ps_release("pR", self.sV, vdr)
                vrc = self.op("vector", lambda e: e.reciprocal(
                    out=sb["drow"][:, 0:NW], in_=sb["drow"][:, 0:NW]))
                vbc = self.mm("pE", self.ps["pE"][:, 0:NW], sb["ones_row_f32"][:],
                              sb["drow"][:, 0:NW], True, True,
                              wait=((self.sV, vrc),))
                vrb = self.op("vector", lambda e: e.tensor_copy(
                    out=sb["rbcast"][:, 0:NW], in_=self.ps["pE"][:, 0:NW]),
                    wait=((self.sT, vbc),))
                self.ps_release("pE", self.sV, vrb)
                ven = self.op("vector", lambda e, tl=tl, kv=kv: e.tensor_tensor(
                    out=ap3(sb["enc_h"], (kv * c.gq) * hw + tl * 128,
                            (hw * c.nh, 128), (hw, c.gq), (1, 128)),
                    in0=ap3(self.ps["pC"], 0, (512, 128), (128, c.gq), (1, 128)),
                    in1=ap3(sb["rbcast"], 0, (NW, 128), (128, c.gq), (1, 128)),
                    op=ALU.mult), wait=((self.sT, self.sT.n),))
                self.ps_release("pC", self.sV, ven)
        enc_done = self.sV.n
        self.dump("enc%d" % h, sb["enc_h"][:], (128, c.nh * hw))
        aow = sb[xnb[0]]
        ao_last = None
        for oc in range(c.nfc):
            self.pump(8)
            b = oc % 2
            wl = self.dma("gpsimd", sb[wsb[b]][:, 0:c.dq], dr["wo"][oc],
                          wait=self.buf_free.get(wsb[b], ()))
            pn = "pA" if oc % 2 == 0 else "pB"
            for hc in range(c.nh):
                w0 = ((self.sG, wl), (self.sV, enc_done)) if hc == 0 else ()
                self.mm(pn, self.ps[pn][:, 0:hw],
                        sb[wsb[b]][:, hc * 128:(hc + 1) * 128],
                        sb["enc_h"][:, hc * hw:(hc + 1) * hw],
                        hc == 0, hc == c.nh - 1, wait=w0)
            ao_last = self.op("vector", lambda e, oc=oc, pn=pn: e.tensor_copy(
                out=aow[:, oc * c.gw:oc * c.gw + hw], in_=self.ps[pn][:, 0:hw]),
                wait=((self.sT, self.sT.n),))
            self.ps_release(pn, self.sV, ao_last)
            self.buf_free[wsb[b]] = ((self.sT, self.sT.n),)
        self.dump("ao%d" % h, aow[:], (128, c.nfc * c.gw))
        vr = self.ln_stats(aow, hw, c.nfc, c.gw, 0, sb["arow_m"][:, 0:hw],
                           sb["arow_r"][:, 0:hw], wait=((self.sV, ao_last),))
        self.bcast_pair(sb["arow_m"][:, 0:hw], sb["arow_r"][:, 0:hw], hw,
                        f32=True, wait=((self.sA, vr),))
        x2h = sb[xnb[1]]
        vlast = None
        for fc in range(c.nfc):
            self.opn("vector", lambda e, fc=fc: e.tensor_tensor(
                out=sb["t1a"][:, 0:hw], in0=aow[:, fc * c.gw:fc * c.gw + hw],
                in1=sb["mbc"][:, 0:hw], op=ALU.subtract),
                wait=(((self.sS, getattr(self, "x2park_dma", 0)),)
                      + self.buf_free.get("x2h_rd", ())) if fc == 0 else ())
            self.opn("vector", lambda e, fc=fc: e.tensor_tensor(
                out=sb["t1b"][:, 0:hw], in0=sb["t1a"][:, 0:hw],
                in1=sb["rbc"][:, 0:hw], op=ALU.mult))
            vlast = self.op("vector", lambda e, fc=fc: e.tensor_tensor(
                out=x2h[:, fc * c.gw:fc * c.gw + hw], in0=sb["t1b"][:, 0:hw],
                in1=xg[:, fc * c.gw:fc * c.gw + hw], op=ALU.add))
        fcg = 4
        for f0 in range(0, c.nfc, fcg):
            self.x2park_dma = self.dma(
                "sync",
                ap3(self.dram["x2park"], f0 * c.tc + h * hw,
                    (c.nfc * c.tc, 128), (c.tc, fcg), (1, hw)),
                ap3(x2h, f0 * c.gw, (c.nfc * c.gw, 128), (c.gw, fcg), (1, hw)),
                wait=((self.sV, vlast),) if f0 == 0 else ())
        self.dump("x2%d" % h, x2h[:], (128, c.nfc * c.gw))
        self.buf_free["xg0"] = ((self.sV, vlast),)
        self.buf_free["xn0_rd"] = ((self.sT, self.sT.n), (self.sA, self.sA.n))
        self.buf_free["x2h_rd"] = ((self.sV, vlast),)

    def phaseF(self, stage=99):
        c, sb, dr = self.c, self.sb, self.dram
        self.xq1_done = 0
        self.bc_done = 0
        self.xn2_ok = 0
        self.pump(len(self.jobs))
        am_done = self.sA.n
        for wi, wname in enumerate(("wg", "wu", "wdn")):
            acc = sb["am_" + wname]
            nck = self.am_nck[wname]
            numel = nck * self.CK * 128
            vred = self.op("vector", lambda e, acc=acc, nck=nck: e.reduce_sum(
                out=sb["col_t"][:, 0:1], in_=acc[:, 0:nck], axis=AX.X),
                wait=((self.sA, am_done),) if wi == 0 else ())
            vtot = self.mm("pR", self.ps["pR"][0:1, 0:1], sb["ones_col_f32"][:],
                           sb["col_t"][:, 0:1], True, True,
                           wait=((self.sV, vred),))
            vs = self.op("vector", lambda e, wi=wi, numel=numel: e.tensor_scalar(
                out=sb["wsc"][:, wi:wi + 1], in0=self.ps["pR"][0:1, 0:1],
                scalar1=1.0 / numel, scalar2=1e-5, op0=ALU.mult, op1=ALU.max),
                wait=((self.sT, vtot),))
            self.ps_release("pR", self.sV, vs)
            self.opn("vector", lambda e, wi=wi: e.tensor_scalar(
                out=sb["wsc"][:, 4 + wi:5 + wi], in0=sb["wsc"][:, wi:wi + 1],
                scalar1=0.5, scalar2=None, op0=ALU.mult),
                wait=((self.sV, vs),))
            vng = self.op("vector", lambda e, wi=wi: e.tensor_scalar(
                out=sb["wsc"][:, 8 + wi:9 + wi], in0=sb["wsc"][:, wi:wi + 1],
                scalar1=-0.5, scalar2=None, op0=ALU.mult))
            vb0 = self.mm("pE", self.ps["pE"][:, 0:1], sb["ones_row_f32"][:],
                          sb["wsc"][:, 4 + wi:5 + wi], True, True,
                          wait=((self.sV, vng),))
            vthr = self.op("vector", lambda e, wi=wi: e.tensor_copy(
                out=sb["thr"][:, wi:wi + 1], in_=self.ps["pE"][:, 0:1]),
                wait=((self.sT, vb0),))
            self.ps_release("pE", self.sV, vthr)
            vb1 = self.mm("pF", self.ps["pF"][:, 0:1], sb["ones_row_f32"][:],
                          sb["wsc"][:, 8 + wi:9 + wi], True, True)
            vthr2 = self.op("vector", lambda e, wi=wi: e.tensor_copy(
                out=sb["thr"][:, 4 + wi:5 + wi], in_=self.ps["pF"][:, 0:1]),
                wait=((self.sT, vb1),))
            self.ps_release("pF", self.sV, vthr2)
        self.thr_done = self.sV.n
        self.dump("wscd", self.sb["wsc"][:], (1, 12))
        self.dump("thrd", self.sb["thr"][:], (128, 8))
        if stage >= 4:
            self.phaseF1_ln()
        if 41 <= stage <= 46:
            self.phaseF2_fwht1(stage)
        elif stage >= 5:
            self.phaseF2_fwht1(99)
        self.P.flush()
        if stage in (5, 6, 99):
            self.phaseF3_gateup()
        self.P.flush()
        if stage in (6, 99):
            for th in range(2):
                self.phaseF5_down(th)
                self.P.flush()

    def phaseF1_ln(self):
        c, sb = self.c, self.sb
        ldx2 = self.dma("sync", sb["fx2"][:], self.dram["x2park"][:],
                        wait=((self.sS, self.x2park_dma),))
        ng = c.tc // c.gw
        self.xn2_ok = 0
        for g in range(ng):
            vr = self.ln_stats(sb["fx2"], c.gw, c.nfc, c.tc, g * c.gw,
                               sb["frow_m"][:, g * c.gw:(g + 1) * c.gw],
                               sb["frow_r"][:, g * c.gw:(g + 1) * c.gw],
                               wait=((self.sS, ldx2),))
            self.bcast_pair(sb["frow_m"][:, g * c.gw:(g + 1) * c.gw],
                            sb["frow_r"][:, g * c.gw:(g + 1) * c.gw], c.gw,
                            f32=True, wait=((self.sA, vr),))
            self.xn2_ok = self.ln_apply(sb["fx2"], sb["fxn2"], c.gw, c.nfc,
                                        c.tc, g * c.gw)
        self.dump("xn2", sb["fxn2"][:], (128, c.nfc * c.tc))

    def phaseF2_fwht1(self, stage=99):
        c, sb, dr = self.c, self.sb, self.dram
        nj1 = 128 // c.q1
        ntg = max(1, c.tc // 512)
        tg_w = min(512, c.tc)
        i = 0
        for fc in range(c.nfc):
            for tg in range(ntg):
                pn = "pE" if i % 2 == 0 else "pF"; i += 1
                vmm = self.mm(pn, self.ps[pn][:, 0:tg_w], sb["c_h128"][:],
                              sb["fxn2"][:, fc * c.tc + tg * tg_w:
                                         fc * c.tc + (tg + 1) * tg_w],
                              True, True,
                              wait=((self.sV, self.xn2_ok),) if i == 1 else ())
                ve = self.op("vector", lambda e, fc=fc, tg=tg, pn=pn: e.tensor_copy(
                    out=sb["fx2"][:, fc * c.tc + tg * tg_w:
                                  fc * c.tc + (tg + 1) * tg_w],
                    in_=self.ps[pn][:, 0:tg_w]), wait=((self.sT, vmm),))
                self.ps_release(pn, self.sV, ve)
        zb_done = self.sV.n
        self.dump("fx2d", self.sb["fx2"][:], (128, self.c.nfc * self.c.tc))
        if stage == 41:
            return
        spitch = c.nfc * c.tc
        dpitch = nj1 * c.tc
        st = self.dma("sync", self.dram["zBpark"][:], sb["fx2"][:],
                      wait=((self.sV, zb_done),))
        shf = 0
        for e in range(c.q1):
            shf = self.dma("sync",
                           ap3(sb["fv2"], (e * c.nfc) * dpitch,
                               (dpitch, c.nfc), (c.tc, nj1), (1, c.tc)),
                           ap3(self.dram["zBpark"], e * spitch,
                               (c.tc, c.nfc), (c.q1 * spitch, nj1), (1, c.tc)),
                           wait=((self.sS, st),) if e == 0 else ())
        self.dump("fv2d", self.sb["fv2"][:], (128, self.c.nfc * self.c.tc))
        if stage == 42:
            return
        i = 0
        for j1 in range(nj1):
            for tau in range(c.ntq):
                pn = "pA" if i % 2 == 0 else "pB"; i += 1
                vmm = self.mm(pn, self.ps[pn][:, 0:128],
                              sb["fv2"][:, j1 * c.tc + tau * 128:
                                        j1 * c.tc + tau * 128 + 128],
                              sb["c_hb1"][:], True, True,
                              wait=((self.sS, shf),) if i == 1 else ())
                self.opn("vector", lambda e, j1=j1, tau=tau, pn=pn: e.reduce_max(
                    out=sb["q1mx"][:, tau * nj1 + j1:tau * nj1 + j1 + 1],
                    in_=self.ps[pn][:, 0:128], axis=AX.X,
                    apply_absolute_value=True), wait=((self.sT, vmm),))
                vev = self.op("vector", lambda e, j1=j1, tau=tau, pn=pn: e.tensor_copy(
                    out=sb["fxn2"][:, tau * c.d + j1 * 128:
                                   tau * c.d + j1 * 128 + 128],
                    in_=self.ps[pn][:, 0:128]))
                self.ps_release(pn, self.sV, vev)
        self.dump("z1d", self.sb["fxn2"][:], (128, self.c.ntq * self.c.d))
        if stage == 43:
            return
        c1 = 1.0 / math.sqrt(c.d)
        vr = 0
        for tau in range(c.ntq):
            vr = self.op("vector", lambda e, tau=tau: e.reduce_max(
                out=sb["q1c"][:, tau:tau + 1],
                in_=sb["q1mx"][:, tau * nj1:(tau + 1) * nj1], axis=AX.X),
                wait=((self.sV, vev),) if tau == 0 else ())
        v1 = self.op("vector", lambda e: e.tensor_scalar(
            out=sb["q1c"][:], in0=sb["q1c"][:], scalar1=c1, scalar2=1e-5,
            op0=ALU.mult, op1=ALU.max), wait=((self.sV, vr),))
        v2 = self.op("vector", lambda e: e.reciprocal(
            out=sb["q1sc"][:], in_=sb["q1c"][:]), wait=((self.sV, v1),))
        v3 = self.op("vector", lambda e: e.tensor_scalar(
            out=sb["q1sc"][:], in0=sb["q1sc"][:], scalar1=127.0 * c1,
            scalar2=None, op0=ALU.mult), wait=((self.sV, v2),))
        vdq = self.op("vector", lambda e: e.tensor_scalar(
            out=sb["q1dq"][:], in0=sb["q1c"][:], scalar1=1.0 / 127.0,
            scalar2=None, op0=ALU.mult), wait=((self.sV, v3),))
        self.dump("q1cd", self.sb["q1c"][:], (128, self.c.ntq))
        self.dump("q1mxd", self.sb["q1mx"][:], (128, self.c.ntq * nj1))
        self.dump("q1scd", self.sb["q1sc"][:], (128, self.c.ntq))
        if stage == 44:
            return
        xq_done = []
        for tau in range(c.ntq):
            wv = ((self.sV, vdq),) if tau == 0 else ((self.sV, xq_done[-1]),)
            va = self.op("scalar", lambda e, tau=tau: e.activation(
                out=sb["fa1"][:], in_=sb["fxn2"][:, tau * c.d:(tau + 1) * c.d],
                func=AF.Copy, scale=sb["q1sc"][:, tau:tau + 1]), wait=wv)
            self.opn("vector", lambda e: e.tensor_copy(
                out=sb["fi1"][:], in_=sb["fa1"][:]), wait=((self.sA, va),))
            vq = self.op("vector", lambda e, tau=tau: e.tensor_copy(
                out=sb["fxn2"][:, tau * c.d:(tau + 1) * c.d], in_=sb["fi1"][:]))
            xq_done.append(vq)
        if stage == 45:
            return
        for tau in range(c.ntq):
            for mb in range(nj1):
                self.dma(
                    "sync",
                    ap2(sb["fxq1"], mb * c.tc + tau * 128,
                        (nj1 * c.tc, 128), (1, 128)),
                    ap2(sb["fxn2"], tau * c.d + mb * 128,
                        (c.ntq * c.d, 128), (1, 128)),
                    wait=((self.sV, xq_done[tau]),) if mb == 0 else (),
                    transpose=True)
        self.xq1_done = self.sS.n
        if stage == 46:
            return
        self.dump("xq1tm", sb["fxn2"][:], (128, c.ntq * c.d))
        self.dump("fxq1", sb["fxq1"][:], (128, (128 // c.q1) * c.tc))
        pk = self.dma("sync",
                      ap2(self.dram["colpark"], 0, (1, 128), (128, c.ntq)),
                      ap2(sb["q1dq"], 0, (c.ntq, 128), (1, c.ntq)),
                      wait=((self.sV, vdq),), allow_slow_non_contiguous=True)
        rr = self.dma("sync", ap2(sb["r1row"], 0, (c.tc, 1), (1, c.tc)),
                      ap2(self.dram["colpark"], 0, (c.tc, 1), (1, c.tc)),
                      wait=((self.sS, pk),))
        vg = self.op("vector", lambda e: e.tensor_scalar(
            out=sb["r1g"][:], in0=sb["r1row"][:], scalar1=sb["wsc"][0:1, 0:1],
            scalar2=None, op0=ALU.mult),
            wait=((self.sS, rr), (self.sV, self.thr_done)))
        vu = self.op("vector", lambda e: e.tensor_scalar(
            out=sb["r1u"][:], in0=sb["r1row"][:], scalar1=sb["wsc"][0:1, 1:2],
            scalar2=None, op0=ALU.mult))
        ntg = max(1, c.tc // 512)
        w = min(512, c.tc)
        for tg in range(ntg):
            v0 = self.mm("pE", self.ps["pE"][:, 0:w], sb["ones_row_f32"][:],
                         sb["r1g"][:, tg * w:(tg + 1) * w], True, True,
                         wait=((self.sV, vu),))
            v1 = self.mm("pF", self.ps["pF"][:, 0:w], sb["ones_row_f32"][:],
                         sb["r1u"][:, tg * w:(tg + 1) * w], True, True)
            va = self.op("vector", lambda e, tg=tg, w=w: e.tensor_copy(
                out=sb["bcg"][:, tg * w:(tg + 1) * w], in_=self.ps["pE"][:, 0:w]),
                wait=((self.sT, v0),))
            vb = self.op("vector", lambda e, tg=tg, w=w: e.tensor_copy(
                out=sb["bcu"][:, tg * w:(tg + 1) * w], in_=self.ps["pF"][:, 0:w]),
                wait=((self.sT, v1),))
            self.ps_release("pE", self.sV, vb)
            self.ps_release("pF", self.sV, vb)
        self.bc_done = self.sV.n

    def phaseF3_gateup(self):
        c, sb, dr = self.c, self.sb, self.dram
        gsb = ("gs0", "gs1"); usb = ("us0", "us1")
        gps = ("pA", "pB"); ups = ("pC", "pD"); hps = ("pE", "pF")
        ntg = max(1, c.tc // 512)
        tg_w = min(512, c.tc)
        for oc in range(c.noc):
            b = oc % 2
            gl = self.dma("gpsimd", sb[gsb[b]][:], dr["wg"][oc],
                          wait=self.buf_free.get(gsb[b], ()))
            ul = self.dma("gpsimd", sb[usb[b]][:], dr["wu"][oc],
                          wait=self.buf_free.get(usb[b], ()))
            vts = {}
            for wname, buf, ld, ti in (("wg", gsb[b], gl, 0), ("wu", usb[b], ul, 1)):
                w0 = ((self.sG, ld), (self.sV, self.thr_done))
                self.opn("vector", lambda e, buf=buf, ti=ti: e.tensor_scalar(
                    out=sb["tmpc"][:], in0=sb[buf][:],
                    scalar1=sb["thr"][:, 4 + ti:5 + ti], scalar2=None,
                    op0=ALU.is_lt), wait=w0)
                vts[wname] = self.op("vector", lambda e, buf=buf, ti=ti:
                                     e.scalar_tensor_tensor(
                    out=sb[buf][:], in0=sb[buf][:],
                    scalar=sb["thr"][:, ti:ti + 1], in1=sb["tmpc"][:],
                    op0=ALU.is_gt, op1=ALU.subtract))
            for tg in range(ntg):
                pn = gps[tg]
                for mc in range(c.nfc):
                    w0 = ((self.sV, vts["wg"]), (self.sS, self.xq1_done)) \
                        if mc == 0 else ()
                    self.mm(pn, self.ps[pn][:, 0:tg_w],
                            sb[gsb[b]][:, mc * 128:(mc + 1) * 128],
                            sb["fxq1"][:, mc * c.tc + tg * tg_w:
                                       mc * c.tc + (tg + 1) * tg_w],
                            mc == 0, mc == c.nfc - 1, wait=w0)
            for tg in range(ntg):
                pn = ups[tg]
                for mc in range(c.nfc):
                    w0 = ((self.sV, vts["wu"]),) if mc == 0 else ()
                    self.mm(pn, self.ps[pn][:, 0:tg_w],
                            sb[usb[b]][:, mc * 128:(mc + 1) * 128],
                            sb["fxq1"][:, mc * c.tc + tg * tg_w:
                                       mc * c.tc + (tg + 1) * tg_w],
                            mc == 0, mc == c.nfc - 1, wait=w0)
            self.buf_free[gsb[b]] = ((self.sT, self.sT.n),)
            self.buf_free[usb[b]] = ((self.sT, self.sT.n),)
            for tg in range(ntg):
                gpn = gps[tg]; upn = ups[tg]
                sl = slice(tg * tg_w, (tg + 1) * tg_w)
                vg1 = self.op("vector", lambda e, gpn=gpn, sl=sl: e.tensor_tensor(
                    out=sb["fgl"][:, sl], in0=self.ps[gpn][:, 0:tg_w],
                    in1=sb["bcg"][:, sl], op=ALU.mult),
                    wait=((self.sT, self.sT.n), (self.sV, self.bc_done)))
                self.ps_release(gpn, self.sV, vg1)
                vsl = self.op("scalar", lambda e, sl=sl: e.activation(
                    out=sb["fsl"][:, sl], in_=sb["fgl"][:, sl], func=AF.Silu),
                    wait=((self.sV, vg1),))
                vu1 = self.op("vector", lambda e, upn=upn, sl=sl: e.tensor_tensor(
                    out=sb["fgl2"][:, sl], in0=self.ps[upn][:, 0:tg_w],
                    in1=sb["bcu"][:, sl], op=ALU.mult))
                self.ps_release(upn, self.sV, vu1)
                vmu = self.op("vector", lambda e, sl=sl: e.tensor_tensor(
                    out=sb["fu"][:, sl], in0=sb["fsl"][:, sl],
                    in1=sb["fgl2"][:, sl], op=ALU.mult),
                    wait=((self.sA, vsl),))
                pn = hps[tg]
                vmm = self.mm(pn, self.ps[pn][:, 0:tg_w], sb["c_h128"][:],
                              sb["fu"][:, sl], True, True,
                              wait=((self.sV, vmu),))
                vz = self.op("vector", lambda e, pn=pn: e.tensor_copy(
                    out=sb["zst"][:, 0:tg_w], in_=self.ps[pn][:, 0:tg_w]),
                    wait=((self.sT, vmm),
                          (self.sS, self.buf_free.get("zst", 0))))
                self.ps_release(pn, self.sV, vz)
                zd = self.dma("sync",
                              self.dram["zApark"][:, oc * c.tc + tg * tg_w:
                                                  oc * c.tc + (tg + 1) * tg_w],
                              sb["zst"][:, 0:tg_w], wait=((self.sV, vz),))
                self.buf_free["zst"] = zd
        self.zApark_done = self.sS.n

    def phaseF5_down(self, th):
        c, sb, dr = self.c, self.sb, self.dram
        hwq = c.hw
        c2 = 1.0 / math.sqrt(c.inner)
        nj2 = 128 // c.q2
        q2 = c.q2
        Dpitch = c.noc * c.tc
        dpitch = nj2 * 128
        neighth = hwq // 128
        for el in range(neighth):
            e_abs = th * neighth + el
            shf = None
            ocg = min(c.noc, max(1, 1024 // nj2))
            first = True
            for bb in range(q2):
                for og in range(0, c.noc, ocg):
                    shf = self.dma(
                        "sync",
                        ap3(sb["fv8"], (bb * c.noc + og) * dpitch, (dpitch, ocg),
                            (128, nj2), (1, 128)),
                        ap3(self.dram["zApark"],
                            bb * Dpitch + og * c.tc + e_abs * 128,
                            (c.tc, ocg), (q2 * Dpitch, nj2), (1, 128)),
                        wait=(self.buf_free.get("fv8", ()) +
                              ((self.sS, self.zApark_done),)) if first else ())
                    first = False
            for j2 in range(nj2):
                pn = "pA" if j2 % 2 == 0 else "pB"
                vmm = self.mm(pn, self.ps[pn][:, 0:128],
                              sb["fv8"][:, j2 * 128:(j2 + 1) * 128],
                              sb["c_hb2"][:], True, True,
                              wait=((self.sS, shf),) if j2 == 0 else ())
                self.opn("vector", lambda e, j2=j2, pn=pn: e.reduce_max(
                    out=sb["q2mx"][:, j2:j2 + 1], in_=self.ps[pn][:, 0:128],
                    axis=AX.X, apply_absolute_value=True),
                    wait=((self.sT, vmm),))
                vev = self.op("vector", lambda e, j2=j2, pn=pn: e.tensor_copy(
                    out=sb["fz8"][:, j2 * 128:(j2 + 1) * 128],
                    in_=self.ps[pn][:, 0:128]),
                    wait=self.buf_free.pop("fz8", ()) if j2 == 0 else ())
                self.ps_release(pn, self.sV, vev)
            self.buf_free["fv8"] = ((self.sT, self.sT.n),)
            w0 = self.buf_free.get("q2dq_rd", ()) if el == 0 else ()
            u0 = self.op("vector", lambda e, el=el: e.reduce_max(
                out=sb["q2c"][:, el:el + 1], in_=sb["q2mx"][:, 0:nj2], axis=AX.X),
                wait=w0 + ((self.sV, vev),))
            u1 = self.op("vector", lambda e, el=el: e.tensor_scalar(
                out=sb["q2c"][:, el:el + 1], in0=sb["q2c"][:, el:el + 1],
                scalar1=c2, scalar2=1e-5, op0=ALU.mult, op1=ALU.max),
                wait=((self.sV, u0),))
            u2 = self.op("vector", lambda e, el=el: e.reciprocal(
                out=sb["q2sc"][:, el:el + 1], in_=sb["q2c"][:, el:el + 1]),
                wait=((self.sV, u1),))
            u3 = self.op("vector", lambda e, el=el: e.tensor_scalar(
                out=sb["q2sc"][:, el:el + 1], in0=sb["q2sc"][:, el:el + 1],
                scalar1=127.0 * c2, scalar2=None, op0=ALU.mult),
                wait=((self.sV, u2),))
            vdq2 = self.op("vector", lambda e, el=el: e.tensor_scalar(
                out=sb["q2dq"][:, el:el + 1], in0=sb["q2c"][:, el:el + 1],
                scalar1=1.0 / 127.0, scalar2=None, op0=ALU.mult),
                wait=((self.sV, u3),))
            self.last_vdq2 = vdq2
            nsl = (c.noc * 128) // 1024
            prev_q = None
            for hs in range(nsl):
                ssl = slice(hs * 1024, (hs + 1) * 1024)
                va = self.op("scalar", lambda e, ssl=ssl, el=el: e.activation(
                    out=sb["fa2"][:], in_=sb["fz8"][:, ssl], func=AF.Copy,
                    scale=sb["q2sc"][:, el:el + 1]),
                    wait=((self.sV, vdq2),) if hs == 0 else ((self.sV, prev_q),))
                self.opn("vector", lambda e: e.tensor_copy(
                    out=sb["fi2"][:], in_=sb["fa2"][:]), wait=((self.sA, va),))
                prev_q = self.op("vector", lambda e, ssl=ssl: e.tensor_copy(
                    out=sb["fz8"][:, ssl], in_=sb["fi2"][:]))
            for j2 in range(nj2):
                self.dma(
                    "sync",
                    ap2(sb["fxq2"], j2 * hwq + el * 128,
                        (nj2 * hwq, 128), (1, 128)),
                    ap2(sb["fz8"], j2 * 128, (nj2 * 128, 128), (1, 128)),
                    wait=((self.sV, prev_q),) if j2 == 0 else (),
                    transpose=True)
            self.buf_free["fz8"] = ((self.sS, self.sS.n),)
        xq2_done = self.sS.n
        if th == 0:
            self.dump("fxq2", sb["fxq2"][:], (128, c.noc * c.hw))
        pk = self.dma("sync",
                      ap2(self.dram["colpark"], 0, (1, 128), (128, neighth)),
                      ap2(sb["q2dq"], 0, (8, 128), (1, neighth)),
                      wait=((self.sV, self.last_vdq2),),
                      allow_slow_non_contiguous=True)
        self.buf_free["q2dq_rd"] = ((self.sS, pk),)
        rr = self.dma("sync", ap2(sb["r2row"], 0, (hwq, 1), (1, hwq)),
                      ap2(self.dram["colpark"], 0, (hwq, 1), (1, hwq)),
                      wait=((self.sS, pk),))
        vgr = self.op("vector", lambda e: e.tensor_scalar(
            out=sb["r2row"][:], in0=sb["r2row"][:], scalar1=sb["wsc"][0:1, 2:3],
            scalar2=None, op0=ALU.mult), wait=((self.sS, rr),))
        vb2 = self.mm("pE", self.ps["pE"][:, 0:hwq], sb["ones_row_f32"][:],
                      sb["r2row"][:], True, True, wait=((self.sV, vgr),))
        vbc2 = self.op("vector", lambda e: e.tensor_copy(
            out=sb["bc2"][:], in_=self.ps["pE"][:, 0:hwq]),
            wait=((self.sT, vb2),))
        self.ps_release("pE", self.sV, vbc2)
        dsb = ("ds0", "ds1")
        for o2 in range(c.nfc):
            pn = "pA" if o2 % 2 == 0 else "pB"
            for hs in range(2):
                b = hs
                ld = self.dma("gpsimd", sb[dsb[b]][:],
                              dr["wdn"][o2][:, hs * (c.inner // 2):
                                            (hs + 1) * (c.inner // 2)],
                              wait=self.buf_free.get(dsb[b], ()))
                w0 = ((self.sG, ld), (self.sV, self.thr_done))
                self.opn("vector", lambda e, b=b: e.tensor_scalar(
                    out=sb["tmpd"][:], in0=sb[dsb[b]][:],
                    scalar1=sb["thr"][:, 6:7], scalar2=None,
                    op0=ALU.is_lt), wait=w0)
                vt = self.op("vector", lambda e, b=b: e.scalar_tensor_tensor(
                    out=sb[dsb[b]][:], in0=sb[dsb[b]][:],
                    scalar=sb["thr"][:, 2:3], in1=sb["tmpd"][:],
                    op0=ALU.is_gt, op1=ALU.subtract))
                nmc = (c.inner // 2) // 128
                for mc in range(nmc):
                    gmc = hs * nmc + mc
                    w1 = ((self.sV, vt),)
                    if o2 == 0 and gmc == 0:
                        w1 = w1 + ((self.sS, xq2_done),)
                    self.mm(pn, self.ps[pn][:, 0:hwq],
                            sb[dsb[b]][:, mc * 128:(mc + 1) * 128],
                            sb["fxq2"][:, gmc * hwq:(gmc + 1) * hwq],
                            gmc == 0, gmc == c.noc - 1,
                            wait=w1 if mc == 0 else ())
                self.buf_free[dsb[b]] = ((self.sT, self.sT.n),)
            vy = self.op("vector", lambda e, o2=o2, pn=pn: e.tensor_tensor(
                out=sb["fy"][:, o2 * hwq:(o2 + 1) * hwq],
                in0=self.ps[pn][:, 0:hwq], in1=sb["bc2"][:], op=ALU.mult),
                wait=((self.sT, self.sT.n), (self.sV, vbc2)) +
                (((self.sA, self.sA.n),) if o2 == 0 else ()))
            self.ps_release(pn, self.sV, vy)
        if th == 0:
            self.dump("q2cd", sb["q2c"][:], (128, 8))
            self.dump("fy", sb["fy"][:], (128, c.nfc * hwq))
        assert hwq <= 512
        vr = self.ln_stats(sb["fy"], hwq, c.nfc, hwq, 0,
                           sb["arow_m"][:, 0:hwq], sb["arow_r"][:, 0:hwq],
                           wait=((self.sV, self.sV.n),))
        self.bcast_pair(sb["arow_m"][:, 0:hwq], sb["arow_r"][:, 0:hwq], hwq,
                        f32=True, wait=((self.sA, vr),))
        fcg = 4
        for f0 in range(0, c.nfc, fcg):
            ldx2 = self.dma(
                "sync",
                ap3(sb["fx2h"], f0 * hwq, (c.nfc * hwq, 128), (hwq, fcg),
                    (1, hwq)),
                ap3(self.dram["x2park"], f0 * c.tc + th * hwq,
                    (c.nfc * c.tc, 128), (c.tc, fcg), (1, hwq)),
                wait=self.buf_free.get("fx2h", ()) if f0 == 0 else ())
        vlast = None
        for fc in range(c.nfc):
            self.opn("vector", lambda e, fc=fc: e.tensor_tensor(
                out=sb["t1a"][:, 0:hwq], in0=sb["fy"][:, fc * hwq:(fc + 1) * hwq],
                in1=sb["mbc"][:, 0:hwq], op=ALU.subtract),
                wait=((self.sS, ldx2), (self.sS, getattr(self, "out_dma", 0)))
                if fc == 0 else ())
            self.opn("vector", lambda e, fc=fc: e.tensor_tensor(
                out=sb["t1b"][:, 0:hwq], in0=sb["t1a"][:, 0:hwq],
                in1=sb["rbc"][:, 0:hwq], op=ALU.mult))
            slot = fc % 2
            vlast = self.op("vector", lambda e, fc=fc, slot=slot: e.tensor_tensor(
                out=sb["fout"][:, slot * hwq:(slot + 1) * hwq],
                in0=sb["t1b"][:, 0:hwq],
                in1=sb["fx2h"][:, fc * hwq:(fc + 1) * hwq], op=ALU.add),
                wait=((self.sS, self.buf_free.get("fout" + str(slot), 0)),))
            self.buf_free["fout" + str(slot)] = self.dma(
                "sync",
                ap2(self.dram["out"], (fc * 128) * c.tc + th * hwq,
                    (c.tc, 128), (1, hwq)),
                sb["fout"][:, slot * hwq:(slot + 1) * hwq],
                wait=((self.sV, vlast),))
        self.out_dma = self.sS.n
        self.buf_free["fx2h"] = ((self.sV, self.sV.n),)


def build_program(cfg, dbg=()):
    b = KBuilder(cfg, dbg=dbg)
    b.build()
    return b


_CACHED = {}


def kernel(L0_attn_x, L0_attn_q_kernel, L0_attn_k_kernel, L0_attn_v_kernel,
           L0_attn_out_kernel, L0_ffn_w_gate, L0_ffn_w_up, L0_ffn_w_down):
    cfg = Cfg()
    x = np.ascontiguousarray(np.asarray(L0_attn_x, dtype=np.float32)[0])
    args = [np.ascontiguousarray(np.asarray(a, dtype=np.float32)) for a in (
        L0_attn_q_kernel, L0_attn_k_kernel, L0_attn_v_kernel,
        L0_attn_out_kernel, L0_ffn_w_gate, L0_ffn_w_up, L0_ffn_w_down)]
    b = build_program(cfg)
    in_maps = [host_prep(cfg, core, x, *args) for core in range(cfg.ncores)]
    res = run_bass_kernel_spmd(b.nc, in_maps, core_ids=list(range(cfg.ncores)))
    out = np.concatenate([np.asarray(r["out"]).T for r in res.results], axis=0)
    return out[None].astype(np.float32)


